# version 01 (final)
# baseline (speedup 1.0000x reference)
import numpy as np
import ml_dtypes
from contextlib import ExitStack

import concourse.bass as bass
import concourse.mybir as mybir
from concourse.bass_utils import run_bass_kernel_spmd

F32 = mybir.dt.float32
BF16 = mybir.dt.bfloat16
I32 = mybir.dt.int32
AF = mybir.ActivationFunctionType
ALU = mybir.AluOpType
AX = mybir.AxisListType
BF16NP = ml_dtypes.bfloat16

T, D, DH, WIN = 8192, 1024, 64, 256
SLA, NSA = 256, 32
SLF, NSF = 512, 16
DI, NI2, REM = 2730, 22, 42
DEPTH = 6
QKS = 8.0
CTX_SCALE = float(np.sqrt(3.0 / 64.0))
WO_O = 8 * 3072


def host_consts():
    c = {}
    c["identb"] = np.eye(128, dtype=np.float32).astype(BF16NP).view(np.uint16)
    c["identf"] = np.eye(128, dtype=np.float32)
    c["onesf"] = np.ones((128, 1), np.float32)
    c["ones1"] = np.ones((1, 128), np.float32)
    NEG = np.float32(-3.0e38)

    def mask_for(w):
        seq = np.arange(T).reshape(NSA, WIN)
        seq_p = np.pad(seq, ((1, 0), (0, 0)), constant_values=-1)
        jt = np.concatenate([seq_p[:NSA], seq_p[1:]], axis=1)
        it, jb = seq[:, :, None], jt[:, None, :]
        return ((it < jb) | (it > jb + WIN) | (jb == -1))[w]
    bg = np.where(mask_for(1), NEG, np.float32(0))
    b0 = np.where(mask_for(0), NEG, np.float32(0))
    c["masks"] = np.stack([bg[0:128], bg[128:256], b0[0:128], b0[128:256]],
                          0).astype(BF16NP).view(np.uint16)

    half = np.arange(0, DH, 2, dtype=np.float64)
    inv_freq = 1.0 / 10000.0 ** (half / DH)
    t = np.arange(T, dtype=np.float64)
    fr = t[:, None] * inv_freq[None, :]
    base_scale = (half + 0.4 * DH) / (1.4 * DH)
    power = (t - T // 2) / (WIN // 2)
    scale = base_scale[None, :] ** power[:, None]
    tabs = np.stack([np.cos(fr) * scale * QKS, np.sin(fr) * scale * QKS,
                     np.cos(fr) / scale, np.sin(fr) / scale], axis=1)
    c["tabs"] = np.ascontiguousarray(
        tabs.astype(np.float32).reshape(NSA, 2, 128, 4, 32).transpose(0, 2, 1, 3, 4))
    return c


def prep_layouts(inp):
    w = {}
    w["wqT"] = np.ascontiguousarray(np.asarray(inp["L0_attn_wq"], np.float32).T)
    w["wkT"] = np.ascontiguousarray(np.asarray(inp["L0_attn_wk"], np.float32).T)
    w["wvT"] = np.ascontiguousarray(np.asarray(inp["L0_attn_wv"], np.float32).T)
    w["woT"] = np.ascontiguousarray(np.asarray(inp["L0_attn_wo"], np.float32).T)
    w["Wqkv6"] = np.ascontiguousarray(np.asarray(inp["L0_ffn_Wqkv"], np.float32))
    w["Wo6"] = np.ascontiguousarray(np.asarray(inp["L0_ffn_Wo"], np.float32))
    w["W16"] = np.ascontiguousarray(np.asarray(inp["L0_ffn_W1"], np.float32))
    W2 = np.asarray(inp["L0_ffn_W2"], np.float32)
    w["W2m"] = np.ascontiguousarray(W2[:, :21 * 128, :])
    w2r = np.zeros((6, 128, D), np.float32)
    w2r[:, :REM, :] = W2[:, 21 * 128:, :]
    w["W2r"] = w2r
    b1 = np.asarray(inp["L0_ffn_b1"], np.float32)
    b1p = np.zeros((6, 2, NI2, 128), np.float32)
    for hf in range(2):
        src = b1[:, hf * DI:(hf + 1) * DI]
        for i in range(NI2):
            n = min(128, DI - i * 128)
            b1p[:, hf, i, :n] = src[:, i * 128:i * 128 + n]
    w["b1p"] = b1p.reshape(6, 44, 128)
    glb = np.stack([np.asarray(inp["L0_ffn_ln_g"], np.float32),
                    np.asarray(inp["L0_ffn_ln_b"], np.float32),
                    np.asarray(inp["L0_ffn_b2"], np.float32)], 1)
    w["glb"] = np.ascontiguousarray(glb.reshape(6, 24, 128))
    w["qsks"] = np.ascontiguousarray(
        np.stack([np.asarray(inp["L0_ffn_q_scale"], np.float32),
                  np.asarray(inp["L0_ffn_k_scale"], np.float32)], 1))
    return w


class Sem:
    def __init__(self, stack, nc, name):
        self.h = stack.enter_context(nc.semaphore(name))
        self.n = 0

    def inc(self, ins, k=1):
        self.n += k
        ins.then_inc(self.h, k)
        return ins


def newton_rsqrt(v, dst, src, tmp, n_iter=3, clamp=None):
    v.tensor_scalar(dst.bitcast(I32), src.bitcast(I32), 1, None, ALU.arith_shift_right)
    ins = v.tensor_scalar(dst.bitcast(I32), dst.bitcast(I32), -1, 0x5F3759DF,
                          ALU.mult, ALU.add)
    for _ in range(n_iter):
        v.tensor_tensor(tmp, dst, dst, ALU.mult)
        v.tensor_tensor(tmp, tmp, src, ALU.mult)
        v.tensor_scalar(tmp, tmp, -0.5, 1.5, ALU.mult, ALU.add)
        ins = v.tensor_tensor(dst, dst, tmp, ALU.mult)
    if clamp is not None:
        ins = v.tensor_scalar(dst, dst, clamp, None, ALU.min)
    return ins


DBG_STAGE = [99]


def build_nc(n_sub=2 * DEPTH, tap_h=False, dbg_attn=0):
    nc = bass.Bass("TRN2", target_bir_lowering=False)
    st = ExitStack()

    x_h = nc.dram_tensor("x", [T, D], F32, kind="ExternalInput").ap()
    wqT_h = nc.dram_tensor("wqT", [D, DH], F32, kind="ExternalInput").ap()
    wkT_h = nc.dram_tensor("wkT", [D, DH], F32, kind="ExternalInput").ap()
    wvT_h = nc.dram_tensor("wvT", [D, DH], F32, kind="ExternalInput").ap()
    woT_h = nc.dram_tensor("woT", [DH, D], F32, kind="ExternalInput").ap()
    wqkv6_h = nc.dram_tensor("Wqkv6", [6, D, 3072], F32, kind="ExternalInput").ap()
    wo6_h = nc.dram_tensor("Wo6", [6, D, D], F32, kind="ExternalInput").ap()
    w16_h = nc.dram_tensor("W16", [6, D, 5460], F32, kind="ExternalInput").ap()
    w2m_h = nc.dram_tensor("W2m", [6, 21 * 128, D], F32, kind="ExternalInput").ap()
    w2r_h = nc.dram_tensor("W2r", [6, 128, D], F32, kind="ExternalInput").ap()
    b1p_h = nc.dram_tensor("b1p", [6, 44, 128], F32, kind="ExternalInput").ap()
    glb_h = nc.dram_tensor("glb", [6, 24, 128], F32, kind="ExternalInput").ap()
    qsks_h = nc.dram_tensor("qsks", [6, 2, DH], F32, kind="ExternalInput").ap()
    identb_h = nc.dram_tensor("identb", [128, 128], mybir.dt.uint16,
                              kind="ExternalInput").ap().bitcast(BF16)
    identf_h = nc.dram_tensor("identf", [128, 128], F32, kind="ExternalInput").ap()
    onesf_h = nc.dram_tensor("onesf", [128, 1], F32, kind="ExternalInput").ap()
    ones1_h = nc.dram_tensor("ones1", [1, 128], F32, kind="ExternalInput").ap()
    masks_h = nc.dram_tensor("masks", [4, 128, 512], mybir.dt.uint16,
                             kind="ExternalInput").ap().bitcast(BF16)
    tabs_h = nc.dram_tensor("tabs", [NSA, 128, 2, 4, 32], F32, kind="ExternalInput").ap()

    out_h = nc.dram_tensor("out", [T, D], F32, kind="ExternalOutput").ap()
    hA = nc.dram_tensor("hA", [D, 1 + T], F32, kind="Internal").ap()
    hB = nc.dram_tensor("hB", [D, 1 + T], F32, kind="Internal").ap()
    qscr = nc.dram_tensor("qscr", [DH, T], F32, kind="Internal").ap()
    uscr = nc.dram_tensor("uscr", [NI2 * 128, T], BF16, kind="Internal").ap()
    tap_ap = (nc.dram_tensor("tap_h", [D, 1 + T], F32, kind="ExternalOutput").ap()
              if tap_h else None)
    dbg_aps = {}
    if dbg_attn:
        for nm, sh, dt in (("tq", [64, 16, SLA], BF16), ("tk", [64, 16, 2, WIN], BF16),
                           ("tv", [128, 16, 4, DH], BF16), ("tao", [128, 8, SLA], BF16),
                           ("tqkrot", [128, 2, 2048], BF16), ("thres", [128, 8, SLA], F32),
                           ("texp", [128, 2, 512], F32), ("tattnb", [128, 2, 512], BF16),
                           ("tattnT", [128, 2, 4, 256], BF16), ("tnm", [128, 2], F32),
                           ("trs", [128, 2], F32), ("tri", [128, 2], F32)):
            dtt = mybir.dt.uint16 if dt == BF16 else dt
            dbg_aps[nm] = nc.dram_tensor(nm, sh, dtt, kind="ExternalOutput").ap()
            if dt == BF16:
                dbg_aps[nm] = dbg_aps[nm].bitcast(BF16)

    hAv = hA.rearrange("(k p) c -> p k c", p=128)
    hBv = hB.rearrange("(k p) c -> p k c", p=128)
    uvv = uscr.rearrange("(k p) c -> p k c", p=128)
    x_sl = x_h.rearrange("(s q p) d -> s p q d", p=128, q=4)
    out_sl = out_h.rearrange("(s q p) d -> s p q d", p=128, q=4)

    sb = lambda name, shape, dt: st.enter_context(nc.sbuf_tensor(name, shape, dt))
    wbig = sb("wbig", [128, 43680], BF16)
    w2rem = sb("w2rem", [128, 1024], BF16)
    identb = sb("identb_s", [128, 128], BF16)
    identf = sb("identf_s", [128, 128], F32)
    onesf = sb("onesf_s", [128, 1], F32)
    ones1 = sb("ones1_s", [1, 128], F32)
    masks = sb("masks_s", [128, 4, 512], BF16)
    qsraw = sb("qsraw", [1, 128], F32)
    qsksb = sb("qsksb", [128, 2, DH], F32)
    glbld = sb("glbld", [24, 128], F32)
    syncsc = sb("syncsc", [1, 128], F32)
    syncv = sb("syncv", [1, 128], F32)
    glbb = sb("glbb", [128, 24], F32)
    b1ld = sb("b1ld", [44, 128], F32)
    b1b = sb("b1b", [128, 44], F32)
    ebf = sb("ebf", [64, DH], BF16)
    wqkvT = sb("wqkvT", [128, 3, 8, DH], BF16)
    woTb = sb("woTb", [64, 8, 128], BF16)
    tab_sb = sb("tab_sb", [128, 2, 2, 4, 32], F32)

    S = lambda name: Sem(st, nc, name)
    s_wt = S("wt"); s_misc = S("misc"); s_lmisc = S("lmisc")
    s_ldhbf = [S("ldhbf0"), S("ldhbf1")]
    s_ldhres = [S("ldhres0"), S("ldhres1")]
    s_ldtab = [S("ldtab0"), S("ldtab1")]
    s_sthp = [S("sthp0"), S("sthp1")]
    s_sth = S("sth"); s_stu = S("stu"); s_stq = S("stq"); s_stout = S("stout")
    s_qkvmm = S("qkvmm"); s_qkev = S("qkev")
    s_pool = [S("pool0"), S("pool1")]
    s_prep = S("prep"); s_tpb = S("tpb"); s_tpev = S("tpev"); s_ring = S("ring")
    s_simmm = S("simmm"); s_nm = S("nm")
    s_simev = [S("simev0"), S("simev1")]
    s_norm = S("norm"); s_attnT = S("attnT"); s_trev = S("trev")
    s_pv = S("pv"); s_aoev = S("aoev"); s_womm = S("womm"); s_resid = S("resid")
    s_f1ldh = [S("f1ldh0"), S("f1ldh1")]; s_sq = S("sq"); s_statmm = S("statmm"); s_rows = S("rows")
    s_bcmm = S("bcmm"); s_hnk = S("hnk"); s_w1mm = S("w1mm"); s_gel = S("gel"); s_ui = S("ui")
    s_sqr = S("sqr")
    s_f2ldu = [S("f2ldu0"), S("f2ldu1")]
    s_f2ldh = [S("f2ldh0"), S("f2ldh1")]
    s_f2mm = S("f2mm"); s_f2res = S("f2res")
    s_ldx = [S("ldx0"), S("ldx1")]; s_ap = S("ap"); s_vf = S("vf"); s_rcp = S("rcp")
    s_cch = S("cch"); s_cmm = S("cmm"); s_csg = S("csg"); s_cadd = S("cadd")
    s_chg = [S("chg0"), S("chg1")]

    def wait_prev(eng, prev, tok_end):
        kind = prev[0]
        if kind == "single":
            _, sem, bse, st_, nsl = prev
            k = min(nsl, -(-tok_end // st_))
            eng.wait_ge(sem.h, bse + 16 * k)
        else:
            _, sems, bses, st_, nsl = prev
            k = min(nsl, -(-tok_end // st_))
            k0 = (k + 1) // 2
            k1 = k // 2
            if k0:
                eng.wait_ge(sems[0].h, bses[0] + 16 * k0)
            if k1:
                eng.wait_ge(sems[1].h, bses[1] + 16 * k1)

    pool_i = [0]
    unit_i = [0]
    uid_c = [0]

    def mk_allocs(es):
        uid_c[0] += 1
        u = uid_c[0]
        sbl = lambda n, sh, d: es.enter_context(nc.sbuf_tensor(f"{n}_{u}", sh, d))
        psl = lambda n, sh, d: es.enter_context(nc.psum_tensor(f"{n}_{u}", sh, d))
        return sbl, psl

    def pool_acquire(eng):
        idx = pool_i[0]
        slot = idx % 2
        if idx >= 2:
            eng.wait_ge(s_pool[slot].h, idx // 2)
        pool_i[0] += 1
        return slot

    def emit_phaseA():
        with ExitStack() as es:
            sbl, psl = mk_allocs(es)
            xa = sbl("xa", [128, 2, 4, D], F32)
            xsq = sbl("xsq", [128, 4, D], F32)
            arow = sbl("arow", [128, 384], F32)
            h0fm = sbl("h0fm", [128, 8, SLF], F32)
            h0bf = sbl("h0bf", [128, 8, SLF], BF16)
            kvbf = sbl("kvbf", [64, 2, SLF], BF16)
            ktm = sbl("ktm", [128, 8, DH], BF16)
            qf = sbl("qf", [64, SLF], F32)
            zrow = sbl("zrow", [128, 8], F32)
            wqkvTf = sbl("wqkvTf", [128, 3, 8, DH], F32)
            woTf = sbl("woTf", [64, 8, 128], F32)
            psE = psl("psE", [64, DH], F32)
            psQKV = psl("psQKV", [64, 3, SLF], F32)
            psT = psl("psT", [128, 2, SLF], F32)
            psK = psl("psK", [128, 8, DH], BF16)
            block = es.enter_context(nc.Block())
            CPS = 9
            b_ap, b_tpb, b_tpev = s_ap.n, s_tpb.n, s_tpev.n
            b_sth, b_stq = s_sth.n, s_stq.n
            b_ldx = [s_ldx[0].n, s_ldx[1].n]

            @block.tensor
            def _(t):
                t.wait_ge(s_lmisc.h, 2)
                for s in range(NSF):
                    b = s % 2
                    t.wait_ge(s_ap.h, b_ap + CPS * s + 3)
                    for bt in range(8):
                        tt, kg = bt // 2, (bt % 2) * 4
                        btg = 8 * s + bt
                        if btg >= 2:
                            t.wait_ge(s_tpev.h, b_tpev + btg - 1)
                        ins = None
                        for j in range(4):
                            ins = t.transpose(psT[:, bt % 2, j * 128:(j + 1) * 128],
                                              xa[:, b, tt, (kg + j) * 128:(kg + j + 1) * 128],
                                              identf[:])
                        s_tpb.inc(ins)
                    t.wait_ge(s_ap.h, b_ap + CPS * s + 4)
                    if s >= 1:
                        t.wait_ge(s_ap.h, b_ap + CPS * (s - 1) + 6)
                    ins = None
                    for a in range(3):
                        for k in range(8):
                            ins = t.matmul(
                                psQKV[:, a], wqkvT[:, a, k],
                                h0bf[:].rearrange("p k c -> p (k c)")[:, k * 512:(k + 1) * 512],
                                start=(k == 0), stop=(k == 7))
                    s_ap.inc(ins)
                    t.wait_ge(s_ap.h, b_ap + CPS * s + 6)
                    if s >= 1:
                        t.wait_ge(s_ap.h, b_ap + CPS * (s - 1) + 8)
                    ins = None
                    for tt in range(4):
                        for a in range(2):
                            ins = t.transpose(psK[:, tt * 2 + a],
                                              kvbf[0:64, a, tt * 128:(tt + 1) * 128],
                                              identb[0:64, 0:64])
                    s_ap.inc(ins)
                    t.wait_ge(s_ap.h, b_ap + CPS * s + 8)
                    ins = None
                    for tt in range(4):
                        ins = t.matmul(psE[:], ktm[:, tt * 2 + 0], ktm[:, tt * 2 + 1],
                                       start=(s == 0 and tt == 0),
                                       stop=(s == NSF - 1 and tt == 3))
                    s_ap.inc(ins)

            @block.scalar
            def _(sc):
                for s in range(NSF):
                    b = s % 2
                    sc.wait_ge(s_ap.h, b_ap + CPS * s + 1)
                    ins = None
                    for tt in range(4):
                        ins = sc.activation(xsq[:, tt], xa[:, b, tt], AF.Square)
                    s_ap.inc(ins)
                    for bt in range(8):
                        tt, kg = bt // 2, (bt % 2) * 4
                        sc.wait_ge(s_tpb.h, b_tpb + 8 * s + bt + 1)
                        if s >= 1 and bt == 0:
                            sc.wait_ge(s_sth.h, b_sth + 16 * s)
                        ins = sc.copy(h0fm[:, kg:kg + 4, tt * 128:(tt + 1) * 128],
                                      psT[:, bt % 2].rearrange("p (k q) -> p k q", k=4))
                        s_tpev.inc(ins)
                    sc.wait_ge(s_tpev.h, s_tpev.n if False else b_tpev + 8 * (s + 1))
                    s_ap.inc(sc.copy(h0bf[:].rearrange("p k c -> p (k c)"),
                                     h0fm[:].rearrange("p k c -> p (k c)")))
                    sc.wait_ge(s_ap.h, b_ap + CPS * s + 7)
                    s_ap.inc(sc.copy(ktm[:].rearrange("p a d -> p (a d)"),
                                     psK[:].rearrange("p a d -> p (a d)")))

            @block.vector
            def _(v):
                v.memset(arow[:], 1.0)
                s_lmisc.inc(v.memset(zrow[:], 0.0))
                v.wait_ge(s_misc.h, 11 * 16)
                v.tensor_copy(wqkvT[:].rearrange("p a k d -> p (a k d)"),
                              wqkvTf[:].rearrange("p a k d -> p (a k d)"))
                s_lmisc.inc(v.tensor_copy(woTb[:].rearrange("p m q -> p (m q)"),
                                          woTf[:].rearrange("p m q -> p (m q)")))
                for s in range(NSF):
                    b = s % 2
                    v.wait_ge(s_ldx[s % 2].h, b_ldx[s % 2] + 16 * (s // 2 + 1))
                    ins = None
                    for tt in range(4):
                        ins = v.reduce_sum(arow[:, tt:tt + 1], xa[:, b, tt], axis=AX.X)
                    s_vf.inc(ins)
                    v.wait_ge(s_vf.h, s_vf.n)
                    s_ap.inc(v.tensor_scalar(arow[:, 0:64], arow[:, 0:64], 1.0 / D,
                                             None, ALU.mult))
                    v.wait_ge(s_ap.h, b_ap + CPS * s + 2)
                    ins = None
                    for tt in range(4):
                        ins = v.reduce_sum(arow[:, 64 + tt:65 + tt], xsq[:, tt], axis=AX.X)
                    s_vf.inc(ins)
                    v.wait_ge(s_vf.h, s_vf.n)
                    v.tensor_scalar(arow[:, 64:128], arow[:, 64:128], 1.0 / D, None, ALU.mult)
                    v.tensor_tensor(arow[:, 128:192], arow[:, 0:64], arow[:, 0:64], ALU.mult)
                    v.tensor_tensor(arow[:, 192:256], arow[:, 64:128], arow[:, 128:192],
                                    ALU.subtract)
                    v.tensor_scalar(arow[:, 192:256], arow[:, 192:256], 1e-5, None, ALU.add)
                    newton_rsqrt(v, arow[:, 256:320], arow[:, 192:256], arow[:, 320:384])
                    ins = None
                    for tt in range(4):
                        ins = v.tensor_scalar(xa[:, b, tt], xa[:, b, tt], arow[:, tt:tt + 1],
                                              arow[:, 256 + tt:257 + tt], ALU.subtract, ALU.mult)
                    s_ap.inc(ins)
                    v.wait_ge(s_ap.h, b_ap + CPS * s + 5)
                    if s >= 1:
                        v.wait_ge(s_stq.h, b_stq + 16 * s)
                    v.tensor_copy(qf[:], psQKV[:, 0])
                    v.tensor_copy(kvbf[0:64, 0], psQKV[:, 1])
                    s_ap.inc(v.tensor_copy(kvbf[0:64, 1], psQKV[:, 2]))
                v.wait_ge(s_ap.h, b_ap + CPS * NSF)
                s_lmisc.inc(v.tensor_scalar(ebf[:], psE[:], CTX_SCALE, None, ALU.mult))

            @block.sync
            def _(sy):
                for t_, src in ((identb, identb_h), (identf, identf_h), (onesf, onesf_h),
                                (ones1, ones1_h)):
                    s_misc.inc(sy.dma_start(out=t_[:], in_=src[:]), 16)
                s_misc.inc(sy.dma_start(out=masks[:], in_=masks_h.rearrange("m p n -> p m n")), 16)
                s_misc.inc(sy.dma_start(out=wqkvTf[:, 0],
                                        in_=wqT_h.rearrange("(k p) d -> p k d", p=128)), 16)
                s_misc.inc(sy.dma_start(out=wqkvTf[:, 1],
                                        in_=wkT_h.rearrange("(k p) d -> p k d", p=128)), 16)
                s_misc.inc(sy.dma_start(out=wqkvTf[:, 2],
                                        in_=wvT_h.rearrange("(k p) d -> p k d", p=128)), 16)
                s_misc.inc(sy.dma_start(out=woTf[:],
                                        in_=woT_h.rearrange("p (m q) -> p m q", m=8)), 16)
                sy.wait_ge(s_lmisc.h, 1)
                with nc.allow_non_contiguous_dma("one zero column, 8 elements per partition"):
                    s_misc.inc(sy.dma_start(out=hAv[:, :, 0:1], in_=zrow[:].unsqueeze(2)), 16)
                    s_misc.inc(sy.dma_start(out=hBv[:, :, 0:1], in_=zrow[:].unsqueeze(2)), 16)
                s_ldx[0].inc(sy.dma_start(out=xa[:, 0], in_=x_sl[0]), 16)
                s_ldx[1].inc(sy.dma_start(out=xa[:, 1], in_=x_sl[1]), 16)
                for s in range(NSF):
                    sy.wait_ge(s_ap.h, b_ap + CPS * s + 4)
                    s_sth.inc(sy.dma_start(out=hAv[:, :, 1 + s * SLF:1 + (s + 1) * SLF],
                                           in_=h0fm[:]), 16)
                    sy.wait_ge(s_ap.h, b_ap + CPS * s + 6)
                    s_stq.inc(sy.dma_start(out=qscr[0:64, s * SLF:(s + 1) * SLF],
                                           in_=qf[:]), 16)
                    if s + 2 < NSF:
                        s_ldx[s % 2].inc(sy.dma_start(out=xa[:, s % 2], in_=x_sl[s + 2]), 16)

            assert s_ap.n == b_ap + CPS * NSF

    def emit_attn(li, src_v, dst_v, prev, nsl=NSA):
        with ExitStack() as es:
            sbl, psl = mk_allocs(es)
            hbf = sbl("hbf", [128, 2, 8, 257], BF16)
            hres = sbl("hres", [128, 2, 8, SLA], F32)
            qk_tm = sbl("qk_tm", [128, 2048], F32)
            qk_sq = sbl("qk_sq", [128, 2048], F32)
            sqred = sbl("sqred", [128, 2, 64], F32)
            inv = sbl("inv", [128, 2, 64], F32)
            invt = sbl("invt", [128, 64], F32)
            rotA = sbl("rotA", [128, 16, 32], F32)
            rotB = sbl("rotB", [128, 16, 32], F32)
            qkrot = sbl("qkrot", [128, 2, 2048], BF16)
            q_fm = sbl("q_fm", [64, 16, SLA], BF16)
            k_fm = sbl("k_fm", [64, 16, 2, WIN], BF16)
            v_tm = sbl("v_tm", [128, 16, 4, DH], BF16)
            negmax = sbl("negmax", [128, 2], F32)
            rsum = sbl("rsum", [128, 2], F32)
            rinv = sbl("rinv", [128, 2], F32)
            exp_sb = sbl("exp_sb", [128, 2, 512], F32)
            attnb = sbl("attnb", [128, 2, 512], BF16)
            attnT = sbl("attnT", [128, 2, 4, 256], BF16)
            ao_fm = sbl("ao_fm", [128, 8, SLA], BF16)
            dbgdump = sbl("dbgdump", [128, 512], BF16)
            ps_big = psl("ps_big", [128, 2, 512], F32)
            ps_tp = psl("ps_tp", [128, 2, 8, 128], BF16)
            ps_sim = psl("ps_sim", [128, 2, 512], F32)
            ps_tr = psl("ps_tr", [128, 2, 4, 128], BF16)
            ps_ao = psl("ps_ao", [64, 2, 256], F32)
            block = es.enter_context(nc.Block())
            B = {k: s.n for k, s in dict(
                wt=s_wt, misc=s_misc, lmisc=s_lmisc, sth=s_sth,
                qkvmm=s_qkvmm, qkev=s_qkev, prep=s_prep,
                tpb=s_tpb, tpev=s_tpev, ring=s_ring, simmm=s_simmm, nm=s_nm,
                norm=s_norm, attnT=s_attnT, trev=s_trev, pv=s_pv, aoev=s_aoev,
                womm=s_womm, resid=s_resid, sqr=s_sqr, rcp=s_rcp).items()}
            Bhbf = [s_ldhbf[0].n, s_ldhbf[1].n]
            Bhres = [s_ldhres[0].n, s_ldhres[1].n]
            Btab = [s_ldtab[0].n, s_ldtab[1].n]
            Bsthp = [s_sthp[0].n, s_sthp[1].n]
            b_u = unit_i[0]
            qkv_slots = []
            wo_slots = []

            @block.tensor
            def _(t):
                if DBG_STAGE[0] < 0.08:
                    return
                t.wait_ge(s_wt.h, B["wt"] + 32)
                if DBG_STAGE[0] >= 0.6:
                    t.wait_ge(s_misc.h, B["misc"] + 16)
                    s_lmisc.inc(t.matmul(ps_sim[:, 0, 0:128], ones1[:], qsraw[:],
                                         start=True, stop=True))
                ST = DBG_STAGE[0]
                for s in range(nsl):
                    b = s % 2
                    if ST < 0.8:
                        break
                    t.wait_ge(s_ldhbf[s % 2].h, Bhbf[s % 2] + 16 * (s // 2 + 1))
                    qkv_slots.append([])
                    for tt in range(2):
                        for c in range(6):
                            if ST >= 0.9:
                                slot = pool_acquire(t)
                            else:
                                slot = 0
                            qkv_slots[s].append(slot)
                            ins = None
                            for k in range(8):
                                c0 = (1 if k < 4 else 0) + tt * 128
                                ins = t.matmul(
                                    ps_big[:, slot],
                                    hbf[:, b, k, c0:c0 + 128],
                                    wbig[:, k * 3072 + c * 512:k * 3072 + (c + 1) * 512],
                                    start=(k == 0), stop=(k == 7))
                            s_qkvmm.inc(ins)
                    for tt in range(2):
                        if ST < 3:
                            continue
                        t.wait_ge(s_prep.h, B["prep"] + 2 * s + tt + 1)
                        for bt in range(4):
                            idx = 8 * s + 4 * tt + bt
                            if idx >= 2:
                                t.wait_ge(s_tpev.h, B["tpev"] + idx - 1)
                            a, hg = bt // 2, (bt % 2) * 8
                            ins = None
                            for j in range(8):
                                ins = t.transpose(
                                    ps_tp[0:64, idx % 2, j],
                                    qkrot[:, tt, (a * 16 + hg + j) * 64:(a * 16 + hg + j + 1) * 64],
                                    identb[:])
                            s_tpb.inc(ins)
                    if ST < 3.3:
                        continue
                    t.wait_ge(s_tpev.h, B["tpev"] + 8 * (s + 1))
                    t.wait_ge(s_ring.h, B["ring"] + s + 1)
                    if s == 0:
                        t.wait_ge(s_lmisc.h, B["lmisc"] + 2)
                    def pe_sim(ux):
                        u = unit_i[0]
                        h, qt = ux // 2, ux % 2
                        if ST == 3.78:
                            if u >= 1:
                                t.wait_ge(s_simev[(u - 1) % 2].h, (u - 1) // 2 + 1)
                            if u >= 2:
                                t.wait_ge(s_simev[u % 2].h, u // 2)
                        elif u >= 2 and ST >= 3.34:
                            t.wait_ge(s_simev[u % 2].h, u // 2)
                        if ST == 3.346 and ux >= 1:
                            t.wait_ge(s_nm.h, B["nm"] + 32 * s + ux)
                        midx = qt if s > 0 else 2 + qt
                        if ST in (3.31, 3.84):
                            s_simmm.inc(t.matmul(ps_sim[:, u % 2],
                                                 q_fm[0:64, h, qt * 128:(qt + 1) * 128],
                                                 k_fm[0:64, h].rearrange("p w n -> p (w n)"),
                                                 start=True, stop=True))
                        elif ST in (3.32, 3.33, 3.34, 3.345, 3.346, 3.82, 3.84):
                            s_simmm.inc(t.matmul(ps_sim[:, u % 2], identb[:],
                                                 masks[:, midx], start=True, stop=True))
                        else:
                            t.matmul(ps_sim[:, u % 2],
                                     q_fm[0:64, h, qt * 128:(qt + 1) * 128],
                                     k_fm[0:64, h].rearrange("p w n -> p (w n)"),
                                     start=True, stop=False)
                            s_simmm.inc(t.matmul(ps_sim[:, u % 2], identb[:],
                                                 masks[:, midx], start=False, stop=True))
                        unit_i[0] += 1

                    def pe_attnT(ux):
                        ul = 32 * s + ux
                        qt = ux % 2
                        t.wait_ge(s_norm.h, B["norm"] + ul + 1)
                        t.wait_ge(s_trev.h, B["trev"] + ul)
                        ins = None
                        for j in range(4):
                            ins = t.transpose(ps_tr[:, qt, j],
                                              attnb[:, qt, j * 128:(j + 1) * 128],
                                              identb[:])
                        s_attnT.inc(ins)

                    def pe_pv(h):
                        t.wait_ge(s_trev.h, B["trev"] + 32 * s + 2 * h + 2)
                        t.wait_ge(s_aoev.h, B["aoev"] + 16 * s + h)
                        ins = None
                        for kt in range(4):
                            ins = t.matmul(ps_ao[0:64, h % 2], v_tm[:, h, kt],
                                           attnT[:, h % 2, kt], start=(kt == 0), stop=(kt == 3))
                        s_pv.inc(ins)

                    for sl in range(38):
                        if sl < 32:
                            pe_sim(sl)
                        if ST >= 4 and 4 <= sl <= 35:
                            pe_attnT(sl - 4)
                        if ST >= 5 and sl >= 7 and sl % 2 == 1 and (sl - 7) // 2 <= 15:
                            pe_pv((sl - 7) // 2)
                    if ST < 6:
                        continue
                    t.wait_ge(s_aoev.h, B["aoev"] + 16 * (s + 1))
                    wo_slots.append([])
                    for m in range(8):
                        slot = pool_acquire(t)
                        wo_slots[s].append(slot)
                        ins = None
                        for k in range(8):
                            ins = t.matmul(
                                ps_big[:, slot, 0:256],
                                wbig[:, WO_O + k * 1024 + m * 128:WO_O + k * 1024 + (m + 1) * 128],
                                ao_fm[:, k], start=(k == 0), stop=(k == 7))
                        s_womm.inc(ins)

            @block.scalar
            def _(sc):
                ST = DBG_STAGE[0]
                for s in range(nsl):
                    if ST < 0.9:
                        break
                    for tt in range(2):
                        for c in range(6):
                            idx = 12 * s + 6 * tt + c
                            sc.wait_ge(s_qkvmm.h, B["qkvmm"] + idx + 1)
                            slot = qkv_slots[s][6 * tt + c]
                            if c < 4:
                                if 2 * s + tt >= 1 and ST >= 2:
                                    sc.wait_ge(s_sqr.h, B["sqr"] + 2 * s + tt)
                                    sc.wait_ge(s_prep.h, B["prep"] + 2 * s + tt)
                                if ST != 0.95:
                                    sc.activation(qk_sq[:, c * 512:(c + 1) * 512],
                                                  ps_big[:, slot], AF.Square)
                                ins = sc.copy(qk_tm[:, c * 512:(c + 1) * 512],
                                              ps_big[:, slot])
                            else:
                                sc.wait_ge(s_ring.h, B["ring"] + s + 1)
                                if ST == 0.9:
                                    ins = sc.copy(dbgdump[:], ps_big[:, slot])
                                else:
                                    ins = sc.copy(
                                        v_tm[:, (c - 4) * 8:(c - 4) * 8 + 8, 2 + tt],
                                        ps_big[:, slot].rearrange("p (h d) -> p h d", d=DH))
                            s_qkev.inc(ins)
                            s_pool[slot].inc(sc.copy(syncsc[0:1, 0:64], syncsc[0:1, 64:128]))
                    for tt in range(2):
                        if ST < 3:
                            continue
                        for bt in range(4):
                            idx = 8 * s + 4 * tt + bt
                            sc.wait_ge(s_tpb.h, B["tpb"] + idx + 1)
                            if bt == 0 and tt == 0:
                                sc.wait_ge(s_ring.h, B["ring"] + s + 1)
                            a, hg = bt // 2, (bt % 2) * 8
                            if a == 0:
                                dst = q_fm[0:64, hg:hg + 8, tt * 128:(tt + 1) * 128]
                            else:
                                dst = k_fm[0:64, hg:hg + 8, 1, tt * 128:(tt + 1) * 128]
                            s_tpev.inc(sc.copy(dst, ps_tp[0:64, idx % 2]))
                    def sc_exp(ul):
                        u = b_u + 32 * s + ul
                        sc.wait_ge(s_nm.h, B["nm"] + 32 * s + ul + 1)
                        if ST == 3.79:
                            sc.activation(exp_sb[:, u % 2], ps_sim[:, u % 2], AF.Exp,
                                          bias=0.0, scale=1.0)
                        elif ST == 3.71:
                            sc.activation(exp_sb[:, u % 2], dbgdump[:], AF.Exp,
                                          bias=negmax[:, u % 2:u % 2 + 1], scale=1.0,
                                          accum_out=rsum[:, u % 2:u % 2 + 1])
                        elif ST == 3.72:
                            sc.copy(exp_sb[:, u % 2], ps_sim[:, u % 2])
                        else:
                            sc.activation(exp_sb[:, u % 2], ps_sim[:, u % 2], AF.Exp,
                                          bias=negmax[:, u % 2:u % 2 + 1], scale=1.0,
                                          accum_out=rsum[:, u % 2:u % 2 + 1])
                        s_simev[u % 2].inc(sc.copy(syncsc[0:1, 0:64], syncsc[0:1, 64:128]))

                    def sc_norm(ul):
                        u = b_u + 32 * s + ul
                        sc.wait_ge(s_rcp.h, B["rcp"] + 32 * s + ul + 1)
                        if ul >= 2 and ST >= 4:
                            sc.wait_ge(s_attnT.h, B["attnT"] + 32 * s + ul - 1)
                        s_norm.inc(sc.activation(attnb[:, u % 2], exp_sb[:, u % 2], AF.Copy,
                                                 bias=0.0, scale=rinv[:, u % 2:u % 2 + 1]))

                    def sc_trev(ux):
                        hh, qt = ux // 2, ux % 2
                        sc.wait_ge(s_attnT.h, B["attnT"] + 32 * s + ux + 1)
                        if hh >= 2 and ST >= 5:
                            sc.wait_ge(s_pv.h, B["pv"] + 16 * s + hh - 1)
                        s_trev.inc(sc.copy(
                            attnT[:, hh % 2, :, qt * 128:(qt + 1) * 128], ps_tr[:, qt]))

                    def sc_aoev(h):
                        sc.wait_ge(s_pv.h, B["pv"] + 16 * s + h + 1)
                        s_aoev.inc(sc.copy(ao_fm[(h % 2) * 64:(h % 2) * 64 + 64, h // 2],
                                           ps_ao[0:64, h % 2]))

                    for sl in range(40):
                        if ST in (3.82, 3.83, 3.84, 3.85, 3.86, 3.87, 3.88, 3.89):
                            if sl < 32:
                                ul = sl
                                u = b_u + 32 * s + ul
                                sc.wait_ge(s_simmm.h, B["simmm"] + 32 * s + ul + 1)
                                if ST >= 3.85:
                                    sc.wait_ge(s_nm.h, B["nm"] + 32 * s + ul + 1)
                                if ST in (3.86, 3.89):
                                    sc.activation(exp_sb[:, u % 2], ps_sim[:, u % 2], AF.Exp,
                                                  bias=negmax[:, u % 2:u % 2 + 1], scale=1.0,
                                                  accum_out=rsum[:, u % 2:u % 2 + 1])
                                elif ST == 3.87:
                                    sc.activation(exp_sb[:, u % 2], ps_sim[:, u % 2], AF.Exp,
                                                  bias=negmax[:, u % 2:u % 2 + 1], scale=1.0)
                                elif ST == 3.88:
                                    sc.activation(exp_sb[:, u % 2], ps_sim[:, u % 2], AF.Exp,
                                                  bias=0.0, scale=1.0)
                                else:
                                    sc.copy(exp_sb[:, u % 2], ps_sim[:, u % 2])
                                s_simev[u % 2].inc(sc.copy(syncsc[0:1, 0:64],
                                                           syncsc[0:1, 64:128]))
                            continue
                        if ST < 3.6:
                            continue
                        if ST >= 4 and 5 <= sl <= 36:
                            sc_trev(sl - 5)
                        if ST >= 3.8 and 2 <= sl <= 33:
                            sc_norm(sl - 2)
                        if sl < 32:
                            sc_exp(sl)
                        if ST < 3.8 and 2 <= sl <= 33:
                            pass
                        if ST >= 5 and sl >= 8 and sl % 2 == 0 and (sl - 8) // 2 <= 15:
                            sc_aoev((sl - 8) // 2)

            @block.vector
            def _(v):
                if DBG_STAGE[0] >= 0.6:
                    v.wait_ge(s_lmisc.h, B["lmisc"] + 1)
                    s_lmisc.inc(v.tensor_copy(qsksb[:].rearrange("p a d -> p (a d)"),
                                              ps_sim[:, 0, 0:128]))
                if DBG_STAGE[0] < 0.06:
                    return
                v.memset(sqred[:, :, 32:64], 1.0)
                v.memset(k_fm[0:64, :, 0], 0.0)
                s_ring.inc(v.memset(v_tm[:, :, 0:2], 0.0))
                ST = DBG_STAGE[0]
                for s in range(nsl):
                    b = s % 2
                    if s >= 1 and ST >= 5:
                        v.wait_ge(s_pv.h, B["pv"] + 16 * s)
                        v.tensor_copy(k_fm[0:64, :, 0], k_fm[0:64, :, 1])
                        s_ring.inc(v.tensor_copy(v_tm[:, :, 0:2], v_tm[:, :, 2:4]))
                    if s >= 1 and ST < 5:
                        s_ring.inc(v.nop())
                    for tt in range(2):
                        if ST < 2:
                            s_prep.inc(v.nop())
                            continue
                        if ST in (3.73,):
                            v.wait_ge(s_qkev.h, B["qkev"] + 12 * s + 6 * tt + 4)
                            s_sqr.inc(v.tensor_copy(syncv[0:1, 0:64], syncv[0:1, 64:128]))
                            s_prep.inc(v.tensor_copy(syncv[0:1, 0:64], syncv[0:1, 64:128]))
                            continue
                        v.wait_ge(s_qkev.h, B["qkev"] + 12 * s + 6 * tt + 4)
                        v.wait_ge(s_ldtab[s % 2].h, Btab[s % 2] + 16 * (s // 2 + 1))
                        if ST >= 3:
                            v.wait_ge(s_tpb.h, B["tpb"] + 8 * s)
                        s_sqr.inc(v.reduce_sum(sqred[:, tt, 0:32].unsqueeze(2),
                                     qk_sq[:].rearrange("p (g d) -> p g d", d=DH),
                                     axis=AX.X))
                        v.wait_ge(s_sqr.h, s_sqr.n)
                        newton_rsqrt(v, inv[:, tt], sqred[:, tt], invt[:], clamp=1e12)
                        v.tensor_tensor(qk_tm[:].rearrange("p (g d) -> p g d", d=DH),
                                        qk_tm[:].rearrange("p (g d) -> p g d", d=DH),
                                        inv[:, tt, 0:32].unsqueeze(2).broadcast_to([128, 32, DH]),
                                        ALU.mult)
                        v.tensor_tensor(
                            qk_tm[:].rearrange("p (a g d) -> p a g d", a=2, d=DH),
                            qk_tm[:].rearrange("p (a g d) -> p a g d", a=2, d=DH),
                            qsksb[:].unsqueeze(2).broadcast_to([128, 2, 16, DH]),
                            ALU.mult)
                        ins = None
                        for a in range(2):
                            cs = tab_sb[:, b, tt, 2 * a]
                            ss = tab_sb[:, b, tt, 2 * a + 1]
                            qv_ = qk_tm[:, a * 1024:(a + 1) * 1024].rearrange(
                                "p (g d) -> p g d", d=DH)
                            dst = qkrot[:, tt, a * 1024:(a + 1) * 1024].rearrange(
                                "p (g d) -> p g d", d=DH)
                            csb = cs.unsqueeze(1).broadcast_to([128, 16, 32])
                            ssb = ss.unsqueeze(1).broadcast_to([128, 16, 32])
                            v.tensor_tensor(rotA[:], qv_[:, :, 32:64], ssb, ALU.mult)
                            v.tensor_tensor(rotB[:], qv_[:, :, 0:32], csb, ALU.mult)
                            v.tensor_tensor(dst[:, :, 0:32], rotB[:], rotA[:], ALU.subtract)
                            v.tensor_tensor(rotA[:], qv_[:, :, 0:32], ssb, ALU.mult)
                            v.tensor_tensor(rotB[:], qv_[:, :, 32:64], csb, ALU.mult)
                            ins = v.tensor_tensor(dst[:, :, 32:64], rotB[:], rotA[:], ALU.add)
                        s_prep.inc(ins)
                    if ST in (3.82, 3.83, 3.84) or ST < 3.34:
                        continue
                    if ST in (3.85, 3.86, 3.87, 3.88, 3.89):
                        for ul85 in range(32):
                            u85 = b_u + 32 * s + ul85
                            v.wait_ge(s_simmm.h, B["simmm"] + 32 * s + ul85 + 1)
                            if ST == 3.89 and u85 >= 2:
                                v.wait_ge(s_simev[u85 % 2].h, u85 // 2)
                            s_nm.inc(v.reduce_max(negmax[:, u85 % 2:u85 % 2 + 1],
                                                  ps_sim[:, u85 % 2], axis=AX.X, negate=True))
                        continue
                    def emit_nm(ul):
                        u = b_u + 32 * s + ul
                        v.wait_ge(s_simmm.h, B["simmm"] + 32 * s + ul + 1)
                        if u >= 2:
                            v.wait_ge(s_simev[u % 2].h, u // 2)
                        if ST == 3.34:
                            ins = v.reduce_max(negmax[:, u % 2:u % 2 + 1],
                                               dbgdump[:], axis=AX.X, negate=True)
                        elif ST == 3.346:
                            ins = v.reduce_max(negmax[:, u % 2:u % 2 + 1],
                                               ps_sim[:, u % 2], axis=AX.X, negate=True)
                        elif ST == 3.345:
                            ins = v.reduce_max(negmax[:, u % 2:u % 2 + 1],
                                               ps_sim[:, u % 2], axis=AX.X, negate=False)
                        else:
                            ins = v.reduce_max(negmax[:, u % 2:u % 2 + 1],
                                               ps_sim[:, u % 2], axis=AX.X, negate=True)
                        s_nm.inc(ins)
                        if ST < 3.6:
                            s_simev[u % 2].inc(v.tensor_copy(syncv[0:1, 0:64],
                                                             syncv[0:1, 64:128]))

                    def emit_rcp(ul):
                        u = b_u + 32 * s + ul
                        v.wait_ge(s_simev[u % 2].h, u // 2 + 1)
                        if ul >= 2:
                            v.wait_ge(s_norm.h, B["norm"] + 32 * s + ul - 1)
                        s_rcp.inc(v.reciprocal(rinv[:, u % 2:u % 2 + 1],
                                               rsum[:, u % 2:u % 2 + 1]))
                    for ul in range(32):
                        emit_nm(ul)
                        if ul >= 1 and ST >= 3.75:
                            emit_rcp(ul - 1)
                    if ST >= 3.75:
                        emit_rcp(31)
                    for m in range(8):
                        if ST < 6:
                            continue
                        v.wait_ge(s_womm.h, B["womm"] + 8 * s + m + 1)
                        if m == 0:
                            v.wait_ge(s_ldhres[s % 2].h, Bhres[s % 2] + 16 * (s // 2 + 1))
                        slot = wo_slots[s][m]
                        ins = v.tensor_tensor(hres[:, b, m], ps_big[:, slot, 0:256],
                                              hres[:, b, m], ALU.add)
                        s_resid.inc(ins)
                        s_pool[slot].inc(v.tensor_copy(syncv[0:1, 0:64], syncv[0:1, 64:128]))

            @block.gpsimd
            def _(g):
                if DBG_STAGE[0] < 0.08:
                    return
                s_wt.inc(g.dma_start(
                    out=wbig[:, 0:WO_O].rearrange("p (k n) -> p k n", k=8),
                    in_=wqkv6_h[li].rearrange("(k p) n -> p k n", p=128)), 16)
                s_wt.inc(g.dma_start(
                    out=wbig[:, WO_O:WO_O + 8 * 1024].rearrange("p (k n) -> p k n", k=8),
                    in_=wo6_h[li].rearrange("(k p) n -> p k n", p=128)), 16)
                for s in range(nsl):
                    if DBG_STAGE[0] < 0.4:
                        break
                    if s >= 2:
                        g.wait_ge(s_qkvmm.h, B["qkvmm"] + 12 * (s - 1))
                    wait_prev(g, prev, s * SLA + 257)
                    s_ldhbf[s % 2].inc(g.dma_start(out=hbf[:, s % 2],
                                                   in_=src_v[:, :, s * SLA:s * SLA + 257]), 16)

            @block.sync
            def _(sy):
                if DBG_STAGE[0] >= 0.6:
                    s_misc.inc(sy.dma_start(
                        out=qsraw[0:1, :],
                        in_=qsks_h[li].rearrange("a d -> (a d)").unsqueeze(0)), 16)
                for s in range(nsl):
                    if DBG_STAGE[0] < 0.3:
                        break
                    if s >= 2 and DBG_STAGE[0] >= 6:
                        sy.wait_ge(s_resid.h, B["resid"] + 8 * (s - 1))
                        sy.wait_ge(s_sthp[s % 2].h, Bsthp[s % 2] + 16 * (s // 2))
                    wait_prev(sy, prev, (s + 1) * SLA)
                    s_ldhres[s % 2].inc(sy.dma_start(
                        out=hres[:, s % 2],
                        in_=src_v[:, :, 1 + s * SLA:1 + (s + 1) * SLA]), 16)
                    if s >= 2:
                        sy.wait_ge(s_prep.h, B["prep"] + 2 * (s - 1))
                    s_ldtab[s % 2].inc(sy.dma_start(out=tab_sb[:, s % 2], in_=tabs_h[s]), 16)
                    if DBG_STAGE[0] >= 6:
                        sy.wait_ge(s_resid.h, B["resid"] + 8 * (s + 1))
                        s_sthp[s % 2].inc(sy.dma_start(
                            out=dst_v[:, :, 1 + s * SLA:1 + (s + 1) * SLA],
                            in_=hres[:, s % 2]), 16)
                if dbg_aps and li == 0 and DBG_STAGE[0] >= 0.35:
                    ST2 = DBG_STAGE[0]
                    if ST2 >= 3:
                        sy.wait_ge(s_tpb.h, B["tpb"] + 8 * nsl)
                    if ST2 >= 5:
                        sy.wait_ge(s_aoev.h, B["aoev"] + 16 * nsl)
                    if ST2 >= 6:
                        sy.wait_ge(s_resid.h, B["resid"] + 8 * nsl)
                    if ST2 >= 0.9:
                        sy.wait_ge(s_qkev.h, B["qkev"] + 12 * nsl)
                    for nm, t_ in (("tq", q_fm), ("tk", k_fm), ("tv", v_tm),
                                   ("tao", ao_fm), ("tqkrot", qkrot), ("texp", exp_sb),
                                   ("tattnb", attnb), ("tattnT", attnT), ("tnm", negmax),
                                   ("trs", rsum), ("tri", rinv)):
                        s_misc.inc(sy.dma_start(out=dbg_aps[nm][:], in_=t_[:]), 16)
                    s_misc.inc(sy.dma_start(out=dbg_aps["thres"][:],
                                            in_=hres[:, (nsl - 1) % 2]), 16)

            assert nsl != NSA or s_qkvmm.n == B["qkvmm"] + 12 * NSA


    def emit_f1(li, src_v, prev):
        with ExitStack() as es:
            sbl, psl = mk_allocs(es)
            h_sb = sbl("h_sb", [128, 2, 8, 513], F32)
            sqb = sbl("sqb", [128, 2, 512], F32)
            rowb = sbl("rowb", [1, 2048], F32)
            tmpk = sbl("tmpk", [128, 2, 512], F32)
            hn = sbl("hn", [128, 8, 512], BF16)
            gel = sbl("gel", [128, 2, 512], F32)
            u_sb = sbl("u_sb", [128, NI2, 512], BF16)
            ps_stat = psl("ps_stat", [1, 512], F32)
            ps_stat2 = psl("ps_stat2", [1, 512], F32)
            ps_bc = psl("ps_bc", [128, 1024], F32)
            ps_f1 = psl("ps_f1", [128, 4, 512], F32)
            block = es.enter_context(nc.Block())
            B = {k: s.n for k, s in dict(
                wt=s_wt, misc=s_misc, lmisc=s_lmisc, sq=s_sq,
                statmm=s_statmm, rows=s_rows, bcmm=s_bcmm, hnk=s_hnk, w1mm=s_w1mm,
                gel=s_gel, ui=s_ui, stu=s_stu, sth=s_sth).items()}
            Bldh = [s_f1ldh[0].n, s_f1ldh[1].n]

            def hsl(s, k):
                off = 1 if k < 4 else 0
                return h_sb[:, s % 2, k, off:off + 512]

            @block.tensor
            def _(t):
                t.wait_ge(s_misc.h, B["misc"] + 32)
                s_lmisc.inc(t.transpose(ps_bc[:, 0:44], b1ld[:], identf[0:44, 0:44]))
                s_lmisc.inc(t.transpose(ps_bc[:, 44:68], glbld[:], identf[0:24, 0:24]))
                t.wait_ge(s_wt.h, B["wt"] + 16)
                for s in range(NSF):
                    if s >= 1:
                        t.wait_ge(s_rows.h, B["rows"] + s)
                    t.wait_ge(s_f1ldh[s % 2].h, Bldh[s % 2] + 16 * (s // 2 + 1))
                    for k in range(8):
                        t.matmul(ps_stat[0:1, :], onesf[:], hsl(s, k),
                                 start=(k == 0), stop=(k == 7))
                    for k in range(8):
                        t.wait_ge(s_sq.h, B["sq"] + 8 * s + k + 1)
                        s_statmm.inc(t.matmul(ps_stat2[0:1, :], onesf[:], sqb[:, k % 2],
                                              start=(k == 0), stop=(k == 7)))
                    t.wait_ge(s_rows.h, B["rows"] + s + 1)
                    if s >= 1:
                        t.wait_ge(s_hnk.h, B["hnk"] + s)
                    else:
                        t.wait_ge(s_lmisc.h, B["lmisc"] + 3)
                    t.matmul(ps_bc[:, 0:512], ones1[:], rowb[0:1, 0:512], start=True, stop=True)
                    s_bcmm.inc(t.matmul(ps_bc[:, 512:1024], ones1[:], rowb[0:1, 1536:2048],
                                        start=True, stop=True))
                    t.wait_ge(s_hnk.h, B["hnk"] + s + 1)
                    for i in range(NI2):
                        wa = min(128, DI - i * 128)
                        for which in range(2):
                            slot = (2 * i + which) % 4
                            if 2 * i + which >= 4:
                                if which == 0:
                                    t.wait_ge(s_ui.h, B["ui"] + NI2 * s + i - 1)
                                else:
                                    t.wait_ge(s_gel.h, B["gel"] + NI2 * s + i - 1)
                            col0 = which * DI + i * 128
                            ins = None
                            for k in range(8):
                                ins = t.matmul(ps_f1[0:wa, slot],
                                               wbig[:, k * 5460 + col0:k * 5460 + col0 + wa],
                                               hn[:, k], start=(k == 0), stop=(k == 7))
                            s_w1mm.inc(ins)

            @block.scalar
            def _(sc):
                for s in range(NSF):
                    for k in range(8):
                        sc.wait_ge(s_f1ldh[s % 2].h, Bldh[s % 2] + 16 * (s // 2 + 1))
                        if 8 * s + k >= 2:
                            sc.wait_ge(s_statmm.h, B["statmm"] + 8 * s + k - 1)
                        s_sq.inc(sc.activation(sqb[:, k % 2], hsl(s, k), AF.Square))
                    for i in range(NI2):
                        wa = min(128, DI - i * 128)
                        gslot = (2 * i + 1) % 4
                        sc.wait_ge(s_w1mm.h, B["w1mm"] + 44 * s + 2 * i + 2)
                        if s == 0 and i == 0:
                            sc.wait_ge(s_lmisc.h, B["lmisc"] + 3)
                        if i >= 2:
                            sc.wait_ge(s_ui.h, B["ui"] + NI2 * s + i - 1)
                        s_gel.inc(sc.activation(gel[0:wa, i % 2], ps_f1[0:wa, gslot],
                                                AF.Gelu, bias=b1b[0:wa, 22 + i:23 + i],
                                                scale=1.0))

            @block.vector
            def _(v):
                v.memset(u_sb[:], 0.0)
                v.wait_ge(s_lmisc.h, B["lmisc"] + 2)
                v.tensor_copy(b1b[:], ps_bc[:, 0:44])
                s_lmisc.inc(v.tensor_copy(glbb[:], ps_bc[:, 44:68]))
                for s in range(NSF):
                    v.wait_ge(s_statmm.h, B["statmm"] + 8 * (s + 1))
                    v.tensor_scalar(rowb[0:1, 0:512], ps_stat[0:1, :], 1.0 / D, None, ALU.mult)
                    v.tensor_scalar(rowb[0:1, 512:1024], ps_stat2[0:1, :],
                                    1.0 / D, None, ALU.mult)
                    v.tensor_tensor(rowb[0:1, 1024:1536], rowb[0:1, 0:512],
                                    rowb[0:1, 0:512], ALU.mult)
                    v.tensor_tensor(rowb[0:1, 512:1024], rowb[0:1, 512:1024],
                                    rowb[0:1, 1024:1536], ALU.subtract)
                    v.tensor_scalar(rowb[0:1, 512:1024], rowb[0:1, 512:1024],
                                    1e-5, None, ALU.add)
                    s_rows.inc(newton_rsqrt(v, rowb[0:1, 1536:2048], rowb[0:1, 512:1024],
                                            rowb[0:1, 1024:1536]))
                    v.wait_ge(s_f1ldh[s % 2].h, Bldh[s % 2] + 16 * (s // 2 + 1))
                    v.wait_ge(s_bcmm.h, B["bcmm"] + s + 1)
                    if s >= 1:
                        v.wait_ge(s_w1mm.h, B["w1mm"] + 44 * s)
                    ins = None
                    for k in range(8):
                        v.tensor_tensor(tmpk[:, k % 2], hsl(s, k), ps_bc[:, 0:512], ALU.subtract)
                        v.tensor_tensor(tmpk[:, k % 2], tmpk[:, k % 2],
                                        ps_bc[:, 512:1024], ALU.mult)
                        ins = v.tensor_scalar(hn[:, k], tmpk[:, k % 2], glbb[:, k:k + 1],
                                              glbb[:, 8 + k:9 + k], ALU.mult, ALU.add)
                    s_hnk.inc(ins)
                    for i in range(NI2):
                        wa = min(128, DI - i * 128)
                        aslot = (2 * i) % 4
                        v.wait_ge(s_gel.h, B["gel"] + NI2 * s + i + 1)
                        if i == 0:
                            v.wait_ge(s_stu.h, B["stu"] + 16 * s)
                        s_ui.inc(v.scalar_tensor_tensor(u_sb[0:wa, i], ps_f1[0:wa, aslot],
                                                        b1b[0:wa, i:i + 1], gel[0:wa, i % 2],
                                                        ALU.add, ALU.mult))

            @block.gpsimd
            def _(g):
                s_wt.inc(g.dma_start(
                    out=wbig[:, 0:8 * 5460].rearrange("p (k n) -> p k n", k=8),
                    in_=w16_h[li].rearrange("(k p) n -> p k n", p=128)), 16)

            @block.sync
            def _(sy):
                s_misc.inc(sy.dma_start(out=b1ld[:], in_=b1p_h[li]), 16)
                s_misc.inc(sy.dma_start(out=glbld[:], in_=glb_h[li]), 16)
                for s in range(NSF):
                    if s >= 2:
                        sy.wait_ge(s_hnk.h, B["hnk"] + s - 1)
                    wait_prev(sy, prev, s * SLF + 513)
                    s_f1ldh[s % 2].inc(sy.dma_start(out=h_sb[:, s % 2],
                                                    in_=src_v[:, :, s * SLF:s * SLF + 513]), 16)
                    sy.wait_ge(s_ui.h, B["ui"] + NI2 * (s + 1))
                    s_stu.inc(sy.dma_start(out=uvv[:, :, s * SLF:(s + 1) * SLF],
                                           in_=u_sb[:]), 16)

            assert s_ui.n == B["ui"] + NI2 * NSF
            assert s_statmm.n == B["statmm"] + 8 * NSF

    def emit_f2(li, src_v, dst_v, prev, f1_stu_base):
        with ExitStack() as es:
            sbl, psl = mk_allocs(es)
            u_in = sbl("u_in", [128, 2, NI2, 512], BF16)
            hres2 = sbl("hres2", [128, 2, 8, 512], F32)
            hpr2 = sbl("hpr2", [128, 8, 512], F32)
            ps_y = psl("ps_y", [128, 2, 512], F32)
            block = es.enter_context(nc.Block())
            B = {k: s.n for k, s in dict(
                wt=s_wt, f2mm=s_f2mm, f2res=s_f2res, sth=s_sth).items()}
            Bldu = [s_f2ldu[0].n, s_f2ldu[1].n]
            Bldh2 = [s_f2ldh[0].n, s_f2ldh[1].n]
            y_slots = []

            @block.tensor
            def _(t):
                if DBG_STAGE[0] < 0.08:
                    return
                t.wait_ge(s_wt.h, B["wt"] + 32)
                for s in range(NSF):
                    b = s % 2
                    t.wait_ge(s_f2ldu[s % 2].h, Bldu[s % 2] + 16 * (s // 2 + 1))
                    y_slots.append([])
                    for m in range(8):
                        slot = pool_acquire(t)
                        y_slots[s].append(slot)
                        for kt in range(21):
                            t.matmul(ps_y[:, slot],
                                     wbig[:, kt * 1024 + m * 128:kt * 1024 + (m + 1) * 128],
                                     u_in[:, b, kt], start=(kt == 0), stop=False)
                        s_f2mm.inc(t.matmul(ps_y[:, slot],
                                            w2rem[0:REM, m * 128:(m + 1) * 128],
                                            u_in[0:REM, b, 21], start=False, stop=True))

            @block.vector
            def _(v):
                for s in range(NSF):
                    b = s % 2
                    for m in range(8):
                        v.wait_ge(s_f2mm.h, B["f2mm"] + 8 * s + m + 1)
                        if m == 0:
                            v.wait_ge(s_f2ldh[s % 2].h, Bldh2[s % 2] + 16 * (s // 2 + 1))
                            v.wait_ge(s_sth.h, B["sth"] + 16 * s)
                        slot = y_slots[s][m]
                        ins = v.scalar_tensor_tensor(hpr2[:, m], ps_y[:, slot],
                                                     glbb[:, 16 + m:17 + m], hres2[:, b, m],
                                                     ALU.add, ALU.add)
                        s_f2res.inc(ins)
                        s_pool[slot].inc(v.tensor_copy(syncv[0:1, 0:64], syncv[0:1, 64:128]))

            @block.gpsimd
            def _(g):
                s_wt.inc(g.dma_start(
                    out=wbig[:, 0:21 * 1024].rearrange("p (k n) -> p k n", k=21),
                    in_=w2m_h[li].rearrange("(k p) n -> p k n", p=128)), 16)
                s_wt.inc(g.dma_start(out=w2rem[:], in_=w2r_h[li]), 16)

            @block.sync
            def _(sy):
                for s in range(NSF):
                    if s >= 2:
                        sy.wait_ge(s_f2mm.h, B["f2mm"] + 8 * (s - 1))
                    sy.wait_ge(s_stu.h, f1_stu_base + 16 * (s + 1))
                    s_f2ldu[s % 2].inc(sy.dma_start(out=u_in[:, s % 2],
                                                    in_=uvv[:, :, s * SLF:(s + 1) * SLF]), 16)
                    if s >= 2:
                        sy.wait_ge(s_f2res.h, B["f2res"] + 8 * (s - 1))
                    wait_prev(sy, prev, (s + 1) * SLF)
                    s_f2ldh[s % 2].inc(sy.dma_start(
                        out=hres2[:, s % 2],
                        in_=src_v[:, :, 1 + s * SLF:1 + (s + 1) * SLF]), 16)
                    sy.wait_ge(s_f2res.h, B["f2res"] + 8 * (s + 1))
                    s_sth.inc(sy.dma_start(out=dst_v[:, :, 1 + s * SLF:1 + (s + 1) * SLF],
                                           in_=hpr2[:]), 16)

            assert s_f2res.n == B["f2res"] + 8 * NSF

    def emit_phaseC(src_v, src_is_A, prev):
        with ExitStack() as es:
            sbl, psl = mk_allocs(es)
            hg = sbl("hg", [128, 2, 8, 512], F32)
            xa2 = sbl("xa2", [128, 2, 4, D], F32)
            qsl = sbl("qsl", [64, 2, 512], F32)
            qb = sbl("qb", [64, 512], BF16)
            ctxb = sbl("ctxb", [64, 512], BF16)
            sg = sbl("sg", [128, 8, 512], F32)
            out_sb = sbl("out_sb", [128, 4, D], F32)
            psCtx = psl("psCtx", [64, 512], F32)
            psS = psl("psS", [128, 2, 512], F32)
            psT2 = psl("psT2", [128, 2, 512], F32)
            block = es.enter_context(nc.Block())
            CPS = 3
            B = dict(cch=s_cch.n, cmm=s_cmm.n, csg=s_csg.n, cadd=s_cadd.n,
                     stout=s_stout.n, sth=s_sth.n, stq=s_stq.n, tpb=s_tpb.n)
            Bhg = [s_chg[0].n, s_chg[1].n]
            Bx2 = [s_ldhres[0].n, s_ldhres[1].n]
            Bq2 = [s_ldtab[0].n, s_ldtab[1].n]

            @block.tensor
            def _(t):
                t.wait_ge(s_lmisc.h, 3)
                for s in range(NSF):
                    t.wait_ge(s_cch.h, B["cch"] + CPS * s + 1)
                    if s >= 1:
                        t.wait_ge(s_cch.h, B["cch"] + CPS * (s - 1) + 3)
                    s_cch.inc(t.matmul(psCtx[:], ebf[:], qb[:], start=True, stop=True))
                    t.wait_ge(s_cch.h, B["cch"] + CPS * s + 3)
                    for m in range(8):
                        if 8 * s + m >= 2:
                            t.wait_ge(s_csg.h, B["csg"] + 8 * s + m - 1)
                        s_cmm.inc(t.matmul(psS[:, m % 2], woTb[0:64, m], ctxb[:],
                                           start=True, stop=True))
                    for bt in range(8):
                        tt, kg = bt // 2, (bt % 2) * 4
                        btg = 8 * s + bt
                        t.wait_ge(s_csg.h, B["csg"] + 8 * s + kg + 4)
                        if btg >= 2:
                            t.wait_ge(s_cadd.h, B["cadd"] + btg - 1)
                        ins = None
                        for j in range(4):
                            ins = t.transpose(psT2[:, bt % 2, j * 128:(j + 1) * 128],
                                              sg[:, kg + j, tt * 128:(tt + 1) * 128],
                                              identf[:])
                        s_tpb.inc(ins)

            @block.scalar
            def _(sc):
                for s in range(NSF):
                    b = s % 2
                    sc.wait_ge(s_cch.h, B["cch"] + CPS * s + 2)
                    sc.copy(ctxb[:], psCtx[:])
                    s_cch.inc(sc.activation(hg[:, b].rearrange("p k c -> p (k c)"),
                                            hg[:, b].rearrange("p k c -> p (k c)"),
                                            AF.Silu))

            @block.vector
            def _(v):
                for s in range(NSF):
                    b = s % 2
                    v.wait_ge(s_chg[s % 2].h, Bhg[s % 2] + 16 * (s // 2 + 1))
                    v.wait_ge(s_ldhres[s % 2].h, Bx2[s % 2] + 16 * (s // 2 + 1))
                    v.wait_ge(s_ldtab[s % 2].h, Bq2[s % 2] + 16 * (s // 2 + 1))
                    if s >= 2:
                        v.wait_ge(s_cadd.h, B["cadd"] + 8 * (s - 1))
                    if s >= 1:
                        v.wait_ge(s_cch.h, B["cch"] + CPS * (s - 1) + 2)
                    s_cch.inc(v.tensor_copy(qb[:], qsl[:, b]))
                    for m in range(8):
                        v.wait_ge(s_cmm.h, B["cmm"] + 8 * s + m + 1)
                        s_csg.inc(v.tensor_tensor(sg[:, m], psS[:, m % 2],
                                                  hg[:, b, m], ALU.mult))
                    for bt in range(8):
                        tt, kg = bt // 2, (bt % 2) * 4
                        btg = 8 * s + bt
                        v.wait_ge(s_tpb.h, B["tpb"] + btg + 1)
                        if bt == 0:
                            v.wait_ge(s_stout.h, B["stout"] + 16 * s)
                        s_cadd.inc(v.tensor_tensor(
                            out_sb[:, tt, kg * 128:(kg + 4) * 128], psT2[:, bt % 2],
                            xa2[:, b, tt, kg * 128:(kg + 4) * 128], ALU.add))

            @block.sync
            def _(sy):
                for s in range(NSF):
                    if s >= 2:
                        sy.wait_ge(s_cch.h, B["cch"] + CPS * (s - 1) + 3)
                        sy.wait_ge(s_csg.h, B["csg"] + 8 * (s - 1))
                    wait_prev(sy, prev, (s + 1) * SLF)
                    s_chg[s % 2].inc(sy.dma_start(
                        out=hg[:, s % 2],
                        in_=src_v[:, :, 1 + s * SLF:1 + (s + 1) * SLF]), 16)
                    if s >= 2:
                        sy.wait_ge(s_cadd.h, B["cadd"] + 8 * (s - 1))
                    s_ldhres[s % 2].inc(sy.dma_start(out=xa2[:, s % 2], in_=x_sl[s]), 16)
                    s_ldtab[s % 2].inc(sy.dma_start(out=qsl[0:64, s % 2],
                                                    in_=qscr[0:64, s * SLF:(s + 1) * SLF]), 16)
                    sy.wait_ge(s_cadd.h, B["cadd"] + 8 * (s + 1))
                    s_stout.inc(sy.dma_start(out=out_sl[s], in_=out_sb[:]), 16)
                if tap_ap is not None:
                    s_stout.inc(sy.dma_start(out=tap_ap[:], in_=(hA if src_is_A else hB)[:]), 16)
                sy.wait_ge(s_stout.h, s_stout.n)

    emit_phaseA()
    src_A = True
    prev = ("single", s_sth, 0, SLF, NSF)
    for ns in range(n_sub):
        li = ns // 2
        src_v = hAv if src_A else hBv
        dst_v = hBv if src_A else hAv
        if ns % 2 == 0:
            bases = [s_sthp[0].n, s_sthp[1].n]
            emit_attn(li, src_v, dst_v, prev, nsl=(dbg_attn or NSA))
            prev = ("pair", s_sthp, bases, SLA, NSA)
            if dbg_attn:
                break
        else:
            f1_stu = s_stu.n
            emit_f1(li, src_v, prev)
            base_before = s_sth.n
            emit_f2(li, src_v, dst_v, prev, f1_stu)
            prev = ("single", s_sth, base_before, SLF, NSF)
        src_A = not src_A
    if not dbg_attn:
        emit_phaseC(hAv if src_A else hBv, src_A, prev)

    with nc.Block() as blockF:
        @blockF.sync
        def _(sy):
            for sem in ([s_wt, s_misc, s_sth, s_stq, s_stu, s_stout] + s_ldx +
                        s_sthp + s_ldhbf + s_ldhres + s_ldtab + s_f1ldh +
                        s_f2ldu + s_f2ldh + s_chg):
                if sem.n:
                    sy.wait_ge(sem.h, sem.n)

    nc.finalize()
    st.close()
    return nc


def kernel(**inputs):
    consts = host_consts()
    w = prep_layouts(inputs)
    x = np.asarray(inputs["L0_attn_x"], np.float32)
    nc = build_nc()
    base = dict(consts)
    base.update(w)
    in_maps = []
    for i in range(8):
        m = dict(base)
        m["x"] = np.ascontiguousarray(x[i])
        in_maps.append(m)
    res = run_bass_kernel_spmd(nc, in_maps, core_ids=list(range(8)))
    return np.stack([res.results[i]["out"] for i in range(8)], 0)
